# version 01 (final)
# baseline (speedup 1.0000x reference)
import numpy as np
from contextlib import ExitStack

import concourse.bass as bass
import concourse.mybir as mybir
from concourse.bass_utils import run_bass_kernel_spmd

F32 = mybir.dt.float32
BF16 = mybir.dt.bfloat16
AF = mybir.ActivationFunctionType
ALU = mybir.AluOpType
AXX = mybir.AxisListType.X

D = 1024
E = 2048
N0 = 64
N1 = 16
NCORES = 8
SEQS = 2
TSEQ = 2048
T_C = 256
DBLK = D // 128
EBLK = E // 128
RND_C = 12582912.0
EPS = 1e-5
DT_MIN, DT_MAX = 0.001, 0.1

ENGINES = ("pe", "act", "dve", "sp", "pool")
DMA_ENG = {"sp", "pool"}


class Buf:
    __slots__ = ("name", "writer", "readers")

    def __init__(self, name):
        self.name = name
        self.writer = None
        self.readers = []


class Op:
    __slots__ = ("eng", "fn", "must_inc", "inc_no", "waits", "tag", "skey", "gseq")

    def __init__(self, eng, fn, tag, skey):
        self.gseq = 0
        self.eng = eng
        self.fn = fn
        self.must_inc = False
        self.inc_no = None
        self.waits = {}
        self.tag = tag
        self.skey = skey


class Em:

    def __init__(self):
        self.ops = {e: [] for e in ENGINES}
        self.skeys = []
        self.gcount = 0
        self.serialize = False

    def emit(self, eng, fn, reads=(), writes=(), tag="", skey=None):
        if skey is None:
            skey = eng
        if skey not in self.skeys and eng in DMA_ENG:
            self.skeys.append(skey)
        op = Op(eng, fn, tag, skey)
        op.gseq = self.gcount
        self.gcount += 1
        self.ops[eng].append(op)
        dma_op = eng in DMA_ENG
        for b in reads:
            if b.writer is not None and (b.writer.eng != eng or dma_op):
                self._need(op, b.writer)
            b.readers.append(op)
        for b in writes:
            if b.writer is not None and (b.writer.eng != eng or dma_op):
                self._need(op, b.writer)
            for r in b.readers:
                if r.eng != eng or dma_op:
                    self._need(op, r)
            b.writer = op
            b.readers = []
        return op

    @staticmethod
    def _need(op, prod):
        prod.must_inc = True
        op.waits.setdefault(prod.skey, []).append(prod)

    def assign(self):
        inc_count = {}
        if self.serialize:
            allops = sorted((o for e in ENGINES for o in self.ops[e]),
                            key=lambda o: o.gseq)
            run = {}
            for op in allops:
                op.must_inc = True
                op.waits = dict(run)
                op.waits.pop(op.skey, None)
                step = 16 if op.eng in DMA_ENG else 1
                run[op.skey] = run.get(op.skey, 0) + step
                op.inc_no = run[op.skey]
            return run
        for e in ENGINES:
            step = 16 if e in DMA_ENG else 1
            for op in self.ops[e]:
                if e in DMA_ENG:
                    op.must_inc = True
                if op.must_inc:
                    inc_count[op.skey] = inc_count.get(op.skey, 0) + step
                    op.inc_no = inc_count[op.skey]
        for e in ENGINES:
            for op in self.ops[e]:
                op.waits = {k: max(p.inc_no for p in prods)
                            for k, prods in op.waits.items()}
        return inc_count

    def lower_engine(self, e, eng, sems, final_waits=()):
        step = 16 if e in DMA_ENG else 1
        hw = {}
        for op in self.ops[e]:
            for k in sorted(op.waits):
                v = op.waits[k]
                if v > hw.get(k, 0):
                    eng.wait_ge(sems[k], v)
                    hw[k] = v
            instr = op.fn(eng)
            if op.must_inc:
                if instr is None:
                    raise RuntimeError(f"op {op.tag} on {e} needs inc, returned None")
                instr.then_inc(sems[op.skey], step)
        for k, v in final_waits:
            if v > hw.get(k, 0):
                eng.wait_ge(sems[k], v)


def pack_lhsT(w_ko):
    K, O = w_ko.shape
    nk, no = K // 128, O // 128
    return np.ascontiguousarray(
        w_ko.reshape(nk, 128, no, 128).transpose(2, 1, 0, 3).reshape(
            no, 128, nk * 128)).astype(np.float32)


def pack_lhsT_narrow(w_ko):
    K, O = w_ko.shape
    return np.ascontiguousarray(
        w_ko.reshape(K // 128, 128, O).transpose(1, 0, 2).reshape(
            128, (K // 128) * O)).astype(np.float32)


def build_program(tseq=TSEQ, tc=T_C, seqs=SEQS, layers=(0, 1), debug=(), serialize=False,
                  dbg_k=0, cut=99):
    nc = bass.Bass("TRN2", target_bir_lowering=False)
    nc.detect_race_conditions = False
    es = ExitStack()
    em = Em()
    em.serialize = serialize
    NCH = tseq // tc
    TP = tc + 2
    HB = 4

    dram = {}

    def din(name, shape, dt=F32):
        dram[name] = nc.dram_tensor(name, list(shape), dt, kind="ExternalInput")

    def dscr(name, shape, dt=F32):
        dram[name] = nc.dram_tensor(name, list(shape), dt)

    din("x", (seqs, D, tseq))
    din("w_inproj", (2 * EBLK, 128, DBLK * 128))
    din("w_dt", (EBLK, 128, EBLK * 128))
    din("w_B", (128, EBLK * N0))
    din("w_C", (128, EBLK * N0))
    din("w_outp", (DBLK, 128, EBLK * 128))
    din("w_gate", (EBLK, 128, DBLK * 128))
    din("w_up", (EBLK, 128, DBLK * 128))
    din("w_down", (DBLK, 128, EBLK * 128))
    din("w_r", (DBLK, 128, DBLK * 128))
    din("w_k", (DBLK, 128, DBLK * 128))
    din("w_v", (DBLK, 128, DBLK * 128))
    din("c_conv_w", (128, EBLK * 4))
    din("c_conv_b", (128, EBLK))
    din("c_dt_b", (128, EBLK))
    din("c_in_norm_w", (128, DBLK))
    din("c_ffn_norm_w", (128, DBLK))
    din("c_mu_r", (128, DBLK))
    din("c_mu_k", (128, DBLK))
    din("a0_logc", (128, 16))
    din("w1_in", (2 * EBLK, 128, DBLK * 128))
    din("w1_x", (128, EBLK * N1))
    din("w1_dt", (EBLK, 128, EBLK * 128))
    din("w1_out", (DBLK, 128, EBLK * 128))
    din("w1_gatef", (2 * EBLK, 128, DBLK * 128))
    din("w1_upf", (2 * EBLK, 128, DBLK * 128))
    din("w1_downf", (DBLK, 128, 2 * EBLK * 128))
    din("c1_conv_w", (128, EBLK * 4))
    din("c1_conv_b", (128, EBLK))
    din("c1_dt_b", (128, EBLK))
    din("c1_D", (128, EBLK))
    din("a1_logc", (128, EBLK * 16))
    din("m_rep4", (128, 4 * 128))
    din("m_sel0", (N0, 16 * 128))
    din("m_sel1", (N1, 4 * 128))
    din("m_p4", (128, 4 * 128))
    din("m_ones_r", (1, 128))
    din("m_ones_c", (128, 1))
    din("m_ident", (128, 128))
    dram["out"] = nc.dram_tensor("out", [seqs, D, tseq], F32, kind="ExternalOutput")
    dscr("s_wq_gate", (2 * EBLK, 128, DBLK * 128), BF16)
    dscr("s_wq_up", (2 * EBLK, 128, DBLK * 128), BF16)
    dscr("s_wq_down", (DBLK, 128, 2 * EBLK * 128), BF16)

    sb_t = {}
    bufs = {}

    def sb(name, shape, dt=F32):
        sb_t[name] = es.enter_context(nc.sbuf_tensor(name, list(shape), dt))
        bufs[name] = Buf(name)

    def psum(name, shape):
        sb_t[name] = es.enter_context(nc.psum_tensor(name, list(shape), F32))
        bufs[name] = Buf(name)

    sb("x_in", (128, DBLK, TP))
    sb("x2", (128, DBLK, tc))
    sb("xn", (128, DBLK, 2 + TP))
    sb("anb", (128, DBLK, TP), BF16)
    sb("scr", (128, DBLK, TP))
    sb("dtmp", (128, 3 * tc))
    sb("xz", (128, EBLK, HB + TP + 2), BF16)
    sb("sz", (128, EBLK, TP), BF16)
    sb("xc", (128, EBLK, tc), BF16)
    sb("bt", (N0, tc), BF16)
    sb("ct", (N0, tc), BF16)
    sb("bq1", (N1, tc), BF16)
    sb("brep", (128, 16, tc), BF16)
    sb("crep", (128, 16, tc), BF16)
    sb("brep1", (128, 4, tc), BF16)
    sb("dte", (128, tc), BF16)
    sb("ue", (128, tc), BF16)
    for p in range(2):
        sb(f"ur4_{p}", (128, 4, tc), BF16)
        sb(f"dA_{p}", (128, 4, tc + 8))
        sb(f"wt_{p}", (128, 4, tc + 8), BF16)
        sb(f"ht_{p}", (128, 4, tc + 8), BF16)
    sb("hcp", (128, 2, 4, tc), BF16)
    sb("yz", (128, EBLK, tc), BF16)
    sb("xatt", (128, DBLK, tc))
    sb("n2b", (128, DBLK, tc), BF16)
    sb("gu", (128, 2 * EBLK, tc), BF16)
    sb("gtmp", (128, tc), BF16)
    sb("gtf", (128, tc))
    sb("xq", (128, DBLK, tc), BF16)
    sb("states0", (128, seqs * EBLK * 16 * 8))
    sb("states1", (128, seqs * EBLK * 4 * 8))
    sb("t_xn", (128, seqs * DBLK * 2))
    sb("t_xz", (128, seqs * EBLK * 2))
    sb("t_xz1", (128, seqs * EBLK * 2))
    sb("rows", (1, 1800))
    sb("amax", (128, 32))
    sb("sinv", (128, 3 * tc))
    sb("wcol", (128, 1))
    sb("wsc", (1, 16))
    sb("rep4", (128, 4 * 128), BF16)
    sb("sel0", (N0, 16 * 128), BF16)
    sb("sel1", (N1, 4 * 128), BF16)
    sb("p4m", (128, 4 * 128), BF16)
    sb("ones_r", (1, 128))
    sb("ones_c", (128, 1))
    sb("identb", (128, 128), BF16)
    sb("identf", (128, 128))
    sb("acols0", (128, 16))
    sb("acols1", (128, EBLK * 4 * 4))
    sb("ccw", (128, EBLK * 4))
    sb("ccb", (128, EBLK))
    sb("cdtb", (128, EBLK))
    sb("cinw", (128, DBLK))
    sb("cffw", (128, DBLK))
    sb("cmur", (128, DBLK))
    sb("comr", (128, DBLK))
    sb("cmuk", (128, DBLK))
    sb("comk", (128, DBLK))
    sb("c1cw", (128, EBLK * 4))
    sb("c1cb", (128, EBLK))
    sb("c1db", (128, EBLK))
    sb("c1D", (128, EBLK))
    for p in range(2):
        sb(f"slab8_{p}", (128, DBLK * 128), BF16)
        sb(f"slab16_{p}", (128, EBLK * 128), BF16)
    sb("slabB", (128, EBLK * N0), BF16)
    sb("slabX1", (128, EBLK * N1), BF16)

    psum("pm0", (128, 512))
    psum("pm1", (128, 512))
    psum("prA", (128, 1024))
    psum("pb0", (128, 512))
    psum("pb1", (128, 512))
    psum("pf", (128, 512))
    psum("px", (128, 512))

    _cvals = (-1.0, 0.5, 1e-5, DT_MAX, DT_MIN, RND_C, -RND_C, 127.0, -128.0,
              1.0 / 127.0, 1.0 / (2 * E * D), -1e-5)
    sb("constv", (128, len(_cvals)))
    for i, v in enumerate(_cvals):
        nc.const_aps.register(F32, v, sb_t["constv"][:, i:i + 1])

    sb_t["mixb"] = sb_t["anb"][:, :, 0:tc]
    bufs["mixb"] = bufs["anb"]
    sb_t["rb"] = sb_t["xq"]
    bufs["rb"] = bufs["xq"]
    sb_t["kact"] = sb_t["n2b"]
    bufs["kact"] = bufs["n2b"]
    sb_t["qtile"] = sb_t["dtmp"][:, 0:512]
    bufs["qtile"] = bufs["dtmp"]
    sb_t["qtile2"] = sb_t["gu"][:, 0:2, :].rearrange("p a b -> p (a b)")
    bufs["qtile2"] = bufs["gu"]
    sb_t["qabs"] = sb_t["scr"][:, :, :].rearrange("p a b -> p (a b)")[:, 0:512]
    bufs["qabs"] = bufs["scr"]
    for _dn in ("d_wq0", "d_wq1", "d_wq2"):
        bufs[_dn] = Buf(_dn)

    B = lambda n: bufs[n]
    T = lambda n: sb_t[n]

    def wslab_ap(wname, o, nk, width=128, ktot=None, koff=0):
        wd = dram[wname]
        ktot = nk if ktot is None else ktot
        return bass.AP(wd, o * 128 * ktot * width + koff * width,
                       [[ktot * width, 128], [1, nk * width]])

    def dma(q, dst, src, reads, writes, tag="", slow=False, skey=None):
        if skey is None:
            skey = writes[0].name if writes else (reads[0].name + "_rd")
        skey = q + "_" + skey
        if slow:
            em.emit(q, lambda g, d=dst, s=src: g.dma_start(
                out=d, in_=s, allow_slow_non_contiguous=True),
                reads=reads, writes=writes, tag=tag, skey="d_" + skey)
        else:
            em.emit(q, lambda g, d=dst, s=src: g.dma_start(out=d, in_=s),
                    reads=reads, writes=writes, tag=tag, skey="d_" + skey)

    def mm(out, lhsT, rhs, reads, writes, start, stop, tag=""):
        em.emit("pe", lambda t, o=out, l=lhsT, r=rhs, a=start, b=stop:
                t.matmul(o, l, r, start=a, stop=b),
                reads=reads, writes=writes, tag=tag)

    def act(out, in_, func, reads, writes, bias=0.0, scale=1.0, tag=""):
        em.emit("act", lambda a, o=out, i=in_, f=func, b=bias, s=scale:
                a.activation(out=o, in_=i, func=f, bias=b, scale=s),
                reads=reads, writes=writes, tag=tag)

    def vtt(out, in0, in1, op, reads, writes, tag=""):
        em.emit("dve", lambda v, o=out, a=in0, b=in1, p=op: v.tensor_tensor(o, a, b, p),
                reads=reads, writes=writes, tag=tag)

    def vts(out, in0, s1, s2, op0, op1, reads, writes, tag=""):
        if s2 is None:
            em.emit("dve", lambda v, o=out, i=in0, a=s1, p=op0:
                    v.tensor_single_scalar(o, i, a, p),
                    reads=reads, writes=writes, tag=tag)
        else:
            em.emit("dve", lambda v, o=out, i=in0, a=s1, b=s2, p=op0, q=op1:
                    v.tensor_scalar(o, i, a, b, p, q),
                    reads=reads, writes=writes, tag=tag)

    def vstt(out, in0, scal, in1, op0, op1, reads, writes, tag=""):
        em.emit("dve", lambda v, o=out, i=in0, s=scal, j=in1, p=op0, q=op1:
                v.scalar_tensor_tensor(o, i, s, j, p, q),
                reads=reads, writes=writes, tag=tag)

    def vcopy(out, in_, reads, writes, tag=""):
        em.emit("dve", lambda v, o=out, i=in_: v.tensor_copy(o, i),
                reads=reads, writes=writes, tag=tag)

    def acopy(out, in_, reads, writes, tag=""):
        em.emit("act", lambda a, o=out, i=in_: a.activation(out=o, in_=i, func=AF.Copy),
                reads=reads, writes=writes, tag=tag)

    def vscan(out, d0, d1, reads, writes, tag=""):
        em.emit("dve", lambda v, o=out, a=d0, b=d1:
                v.tensor_tensor_scan(o, a, b, 0.0, ALU.mult, ALU.add),
                reads=reads, writes=writes, tag=tag)

    def vmemset(ap, val, writes, tag=""):
        em.emit("dve", lambda v, a=ap, x=val: v.memset(a, x), writes=writes, tag=tag)

    def vreduce(out, in_, op, reads, writes, absval=False, tag=""):
        em.emit("dve", lambda v, o=out, i=in_, p=op, av=absval:
                v.tensor_reduce(o, i, AXX, p, av),
                reads=reads, writes=writes, tag=tag)

    def vrecip(out, in_, reads, writes, tag=""):
        em.emit("dve", lambda v, o=out, i=in_: v.reciprocal(o, i),
                reads=reads, writes=writes, tag=tag)

    def transp(out, in_, reads, writes, tag="", f32=False):
        idn = "identf" if f32 else "identb"
        em.emit("pe", lambda t, o=out, i=in_, n=idn: t.transpose(o, i, T(n)[:, :]),
                reads=list(reads) + [B(idn)], writes=writes, tag=tag)

    for i, v in enumerate(_cvals):
        vmemset(T("constv")[:, i:i + 1], v, [B("constv")], tag=f"const_{i}")
    dma("pool", T("rep4")[:, :], dram["m_rep4"][:, :], [], [B("rep4")])
    dma("pool", T("sel0")[:, :], dram["m_sel0"][:, :], [], [B("sel0")])
    dma("pool", T("sel1")[:, :], dram["m_sel1"][:, :], [], [B("sel1")])
    dma("pool", T("p4m")[:, :], dram["m_p4"][:, :], [], [B("p4m")])
    dma("sp", T("ones_r")[:, :], dram["m_ones_r"][:, :], [], [B("ones_r")])
    dma("sp", T("ones_c")[:, :], dram["m_ones_c"][:, :], [], [B("ones_c")])
    dma("pool", T("identb")[:, :], dram["m_ident"][:, :], [], [B("identb")])
    dma("sp", T("identf")[:, :], dram["m_ident"][:, :], [], [B("identf")])
    for s_, d_ in [("ccw", "c_conv_w"), ("ccb", "c_conv_b"), ("cdtb", "c_dt_b"),
                   ("cinw", "c_in_norm_w"), ("cffw", "c_ffn_norm_w"),
                   ("cmur", "c_mu_r"), ("cmuk", "c_mu_k"),
                   ("c1cw", "c1_conv_w"), ("c1cb", "c1_conv_b"),
                   ("c1db", "c1_dt_b"), ("c1D", "c1_D")]:
        dma("sp", T(s_)[:, :], dram[d_][:, :], [], [B(s_)])
    vts(T("comr")[:, :], T("cmur")[:, :], -1.0, 1.0, ALU.mult, ALU.add,
        [B("cmur")], [B("comr")])
    vts(T("comk")[:, :], T("cmuk")[:, :], -1.0, 1.0, ALU.mult, ALU.add,
        [B("cmuk")], [B("comk")])

    dma("sp", T("acols0")[:, :], dram["a0_logc"][:, :], [], [B("acols0")])
    act(T("acols0")[:, :], T("acols0")[:, :], AF.Exp, [B("acols0")], [B("acols0")])
    vts(T("acols0")[:, :], T("acols0")[:, :], -1.0, None, ALU.mult, None,
        [B("acols0")], [B("acols0")])
    dma("sp", T("acols1")[:, :], dram["a1_logc"][:, :], [], [B("acols1")])
    act(T("acols1")[:, :], T("acols1")[:, :], AF.Exp, [B("acols1")], [B("acols1")])
    vts(T("acols1")[:, :], T("acols1")[:, :], -1.0, None, ALU.mult, None,
        [B("acols1")], [B("acols1")])

    if 1 in layers:
        for wi, (wname, qname) in enumerate([("w1_gatef", "s_wq_gate"),
                                             ("w1_upf", "s_wq_up"),
                                             ("w1_downf", "s_wq_down")]):
            wd = dram[wname]
            no, ncol = wd.shape[0], wd.shape[2]
            n_el = no * 128 * ncol
            ntile = no * ncol // 512
            tile_ap = lambda hd, it_: bass.AP(
                hd, (it_ // (ncol // 512)) * 128 * ncol + (it_ % (ncol // 512)) * 512,
                [[ncol, 128], [1, 512]])
            for it in range(ntile):
                dma("sp", T("qtile")[:, :], tile_ap(wd, it), [], [B("qtile")],
                    tag=f"q1{wi}_{it}")
                act(T("qabs")[:, :], T("qtile")[:, :], AF.Abs, [B("qtile")], [B("qabs")])
                mm(T("px")[:1, :], T("ones_c")[:, :], T("qabs")[:, :],
                   [B("qabs"), B("ones_c")], [B("px")],
                   start=(it == 0), stop=(it == ntile - 1))
            vreduce(T("wsc")[:1, 0:1], T("px")[:1, :], ALU.add, [B("px")], [B("wsc")])
            act(T("wsc")[:1, 1:2], T("wsc")[:1, 0:1], AF.Relu, [B("wsc")], [B("wsc")],
                bias=-1e-5, scale=1.0 / n_el)
            vts(T("wsc")[:1, 4:5], T("wsc")[:1, 1:2], 1e-5, None, ALU.add, None,
                [B("wsc")], [B("wsc")])
            act(T("wsc")[:1, 5:6], T("wsc")[:1, 4:5], AF.Ln, [B("wsc")], [B("wsc")])
            vts(T("wsc")[:1, 6:7], T("wsc")[:1, 5:6], -1.0, None, ALU.mult, None,
                [B("wsc")], [B("wsc")])
            act(T("wsc")[:1, 2:3], T("wsc")[:1, 6:7], AF.Exp, [B("wsc")], [B("wsc")])
            acopy(T("wsc")[:1, 8 + wi:9 + wi], T("wsc")[:1, 4:5], [B("wsc")], [B("wsc")])
            mm(T("px")[:, 0:1], T("ones_r")[:1, :], T("wsc")[:1, 2:3],
               [B("wsc"), B("ones_r")], [B("px")], start=True, stop=True)
            act(T("wcol")[:, :], T("px")[:, 0:1], AF.Copy, [B("px")], [B("wcol")])
            for it in range(ntile):
                dma("sp", T("qtile")[:, :], tile_ap(wd, it), [], [B("qtile")],
                    tag=f"q2{wi}_{it}")
                vts(T("qtile")[:, :], T("qtile")[:, :], T("wcol")[:, 0:1], None,
                    ALU.mult, None, [B("qtile"), B("wcol")], [B("qtile")])
                vts(T("qtile")[:, :], T("qtile")[:, :], RND_C, -RND_C, ALU.add, ALU.add,
                    [B("qtile")], [B("qtile")])
                vts(T("qtile2")[:, :], T("qtile")[:, :], 1.0, -1.0, ALU.min, ALU.max,
                    [B("qtile")], [B("qtile2")])
                dma("pool", tile_ap(dram[qname], it), T("qtile2")[:, :],
                    [B("qtile2")], [B(f"d_wq{wi}")], tag=f"q3{wi}_{it}",
                    skey="wq" + str(wi))

    vmemset(T("states0")[:, :], 0.0, [B("states0")])
    vmemset(T("states1")[:, :], 0.0, [B("states1")])
    vmemset(T("t_xn")[:, :], 0.0, [B("t_xn")])
    vmemset(T("t_xz")[:, :], 0.0, [B("t_xz")])
    vmemset(T("t_xz1")[:, :], 0.0, [B("t_xz1")])
    for p in range(2):
        vmemset(T(f"dA_{p}")[:, :, 0:4], 0.0, [B(f"dA_{p}")])
        vmemset(T(f"dA_{p}")[:, :, tc + 4:tc + 8], 0.0, [B(f"dA_{p}")])
        vmemset(T(f"wt_{p}")[:, :, 0:4], 0.0, [B(f"wt_{p}")])
        vmemset(T(f"wt_{p}")[:, :, tc + 4:tc + 8], 0.0, [B(f"wt_{p}")])

    n_store = [0]

    def ln_stage(src_name, tp, rms_gain):
        X = T(src_name)
        xb = B(src_name)
        act(T("scr")[:, :, :tp], X[:, :, :tp], AF.Square, [xb], [B("scr")])
        for d in range(DBLK):
            mm(T("pb1")[:1, :tp], T("ones_c")[:, :], X[:, d, :tp],
               [xb, B("ones_c")], [B("pb1")], start=(d == 0), stop=(d == DBLK - 1))
        for d in range(DBLK):
            mm(T("pb0")[:1, :tp], T("ones_c")[:, :], T("scr")[:, d, :tp],
               [B("scr"), B("ones_c")], [B("pb0")], start=(d == 0), stop=(d == DBLK - 1))
        R = T("rows")
        rb_ = B("rows")
        act(R[:1, 0:tp], T("pb1")[:1, :tp], AF.Copy, [B("pb1")], [rb_], scale=1.0 / D)
        act(R[:1, 512:512 + tp], T("pb0")[:1, :tp], AF.Copy, [B("pb0")], [rb_],
            scale=1.0 / D)
        vtt(R[:1, 1024:1024 + tp], R[:1, 0:tp], R[:1, 0:tp], ALU.mult, [rb_], [rb_])
        vtt(R[:1, 1024:1024 + tp], R[:1, 512:512 + tp], R[:1, 1024:1024 + tp],
            ALU.subtract, [rb_], [rb_])
        act(R[:1, 1536:1536 + tp], R[:1, 1024:1024 + tp], AF.Ln, [rb_], [rb_], bias=EPS)
        act(R[:1, 1536:1536 + tp], R[:1, 1536:1536 + tp], AF.Exp, [rb_], [rb_], scale=-0.5)
        vts(R[:1, 0:tp], R[:1, 0:tp], -1.0, None, ALU.mult, None, [rb_], [rb_])
        mm(T("pb0")[:, :tp], T("ones_r")[:1, :], R[:1, 0:tp], [rb_, B("ones_r")],
           [B("pb0")], start=True, stop=True)
        mm(T("px")[:, :tp], T("ones_r")[:1, :], R[:1, 1536:1536 + tp], [rb_, B("ones_r")],
           [B("px")], start=True, stop=True)
        XN = T("xn")
        vtt(XN[:, :, 2:2 + tp], X[:, :, :tp],
            T("pb0")[:, :tp].unsqueeze(1).broadcast_to([128, DBLK, tp]), ALU.add,
            [xb, B("pb0")], [B("xn")])
        vtt(XN[:, :, 2:2 + tp], XN[:, :, 2:2 + tp],
            T("px")[:, :tp].unsqueeze(1).broadcast_to([128, DBLK, tp]), ALU.mult,
            [B("xn"), B("px")], [B("xn")])
        if rms_gain is None:
            vcopy(T("anb")[:, :, :tp], XN[:, :, 2:2 + tp], [B("xn")], [B("anb")])
        else:
            vts(R[:1, 512:512 + tp], R[:1, 1024:1024 + tp], EPS, None, ALU.add, None,
                [rb_], [rb_])
            vrecip(R[:1, 512:512 + tp], R[:1, 512:512 + tp], [rb_], [rb_])
            vtt(R[:1, 512:512 + tp], R[:1, 1024:1024 + tp], R[:1, 512:512 + tp], ALU.mult,
                [rb_], [rb_])
            act(R[:1, 512:512 + tp], R[:1, 512:512 + tp], AF.Ln, [rb_], [rb_], bias=EPS)
            act(R[:1, 512:512 + tp], R[:1, 512:512 + tp], AF.Exp, [rb_], [rb_], scale=-0.5)
            mm(T("pb0")[:, :tp], T("ones_r")[:1, :], R[:1, 512:512 + tp],
               [rb_, B("ones_r")], [B("pb0")], start=True, stop=True)
            for d in range(DBLK):
                vstt(T("anb")[:, d, :tp], XN[:, d, 2:2 + tp], T(rms_gain)[:, d:d + 1],
                     T("pb0")[:, :tp], ALU.mult, ALU.mult,
                     [B("xn"), B(rms_gain), B("pb0")], [B("anb")])

    _wq_src = {"s_wq_gate": "d_wq0", "s_wq_up": "d_wq1", "s_wq_down": "d_wq2"}

    def matmul_op(wname, nk, no, rhs_fn, evac_fn, slab, tag, wq=False, width=128):
        for o in range(no):
            sl = slab[o % 2] if isinstance(slab, tuple) else slab
            slb = B(sl)
            q = "sp" if wq else "pool"
            rbufs = [B(_wq_src[wname])] if wname in _wq_src else []
            dma(q, T(sl)[:, :nk * width], wslab_ap(wname, o, nk, width),
                rbufs, [slb], tag=f"w_{tag}_{o}")
            pm = ("pm0", "pm1")[o % 2]
            for kk in range(nk):
                rhs_ap, rhs_bufs = rhs_fn(kk)
                mm(T(pm)[:, :rhs_ap.shape[-1]], T(sl)[:, kk * width:kk * width + width],
                   rhs_ap, [slb] + rhs_bufs, [B(pm)],
                   start=(kk == 0), stop=(kk == nk - 1), tag=f"mm_{tag}_{o}_{kk}")
            evac_fn(o, pm)

    dbg_scan = [0]
    k_dbg = [0]

    def scan_block(e, s, tc_, layer, up):
        ng_g = 16 if layer == 0 else 4
        states = "states0" if layer == 0 else "states1"
        for g in range(ng_g):
            par = g % 2
            dA, wt_, ht = T(f"dA_{par}"), T(f"wt_{par}"), T(f"ht_{par}")
            dab, wtb, htb = B(f"dA_{par}"), B(f"wt_{par}"), B(f"ht_{par}")
            if layer == 0:
                act(dA[:, :, 4:4 + tc_],
                    bass.AP(sb_t["prA"], 0, [[1024, 128], [256, 4], [1, tc_]]),
                    AF.Exp, [B("prA"), B("acols0")], [dab],
                    scale=T("acols0")[:, g:g + 1], tag=f"exp_{e}_{g}")
            else:
                for j in range(4):
                    act(dA[:, j, 4:4 + tc_], T("prA")[:, 256 * j:256 * j + tc_],
                        AF.Exp, [B("prA"), B("acols1")], [dab],
                        scale=T("acols1")[:, (e * 4 + j) * 4 + g:(e * 4 + j) * 4 + g + 1],
                        tag=f"exp1_{e}_{g}_{j}")
            brep_ap = (T("brep")[:, g, :tc_] if layer == 0 else T("brep1")[:, g, :tc_])
            brep_b = B("brep") if layer == 0 else B("brep1")
            vtt(wt_[:, :, 4:4 + tc_], T(up)[:, :, :tc_],
                brep_ap.unsqueeze(1).broadcast_to([128, 4, tc_]), ALU.mult,
                [B(up), brep_b], [wtb], tag=f"w_{e}_{g}")
            stoff = ((s * EBLK + e) * ng_g + g) * 8
            st0 = T(states)[:, stoff:stoff + 8].rearrange("p (a b) -> p a b", a=4)
            acopy(wt_[:, :, 0:4].bitcast(F32), st0, [B(states)], [wtb])
            vscan(ht[:, :, :].rearrange("p a b -> p (a b)"),
                  dA[:, :, :].rearrange("p a b -> p (a b)"),
                  wt_[:, :, :].rearrange("p a b -> p (a b)"),
                  [dab, wtb], [htb], tag=f"scan_{e}_{g}")
            acopy(st0, ht[:, :, tc_:tc_ + 4].bitcast(F32), [htb], [B(states)])
            if debug and dbg_scan[0] < 2 and layer == 0 and e == 0 and k_dbg[0] == dbg_k:
                i = dbg_scan[0]
                dbg_scan[0] += 1
                for nm2, tt in [("sdA", dA), ("swt", wt_), ("sht", ht)]:
                    dd = nc.dram_tensor(f"dbg_{nm2}{i}", [128, 4, tc_ + 8],
                                        tt.dtype if hasattr(tt, "dtype") else F32,
                                        kind="ExternalOutput")
                    dma("pool", dd[:, :, :], tt[:, :, :],
                        [B(f"dA_{par}") if nm2 == "sdA" else
                         B(f"wt_{par}") if nm2 == "swt" else B(f"ht_{par}")], [],
                        tag=f"dbgscan_{nm2}{i}", skey="out")
                dd = nc.dram_tensor(f"dbg_sur{i}", [128, 4, tc_], BF16,
                                    kind="ExternalOutput")
                dma("pool", dd[:, :, :], T(up)[:, :, :tc_], [B(up)], [],
                    tag=f"dbgsur{i}", skey="out")
                dd = nc.dram_tensor(f"dbg_stsnap{i}", [128, 128], F32,
                                    kind="ExternalOutput")
                dma("pool", dd[:, :], T(states)[:, :128], [B(states)], [],
                    tag=f"dbgstsnap{i}", skey="out")
            if layer == 0:
                vtt(T("hcp")[:, par, :, :tc_], ht[:, :, 4:4 + tc_],
                    T("crep")[:, g, :tc_].unsqueeze(1).broadcast_to([128, 4, tc_]),
                    ALU.mult, [htb, B("crep")], [B("hcp")], tag=f"hc_{e}_{g}")
            else:
                vcopy(T("hcp")[:, par, :, :tc_], ht[:, :, 4:4 + tc_], [htb], [B("hcp")],
                      tag=f"hcc_{e}_{g}")
            for j in range(4):
                mm(T("pf")[:, :tc_], T("p4m")[:, j * 128:(j + 1) * 128],
                   T("hcp")[:, par, j, :tc_],
                   [B("p4m"), B("hcp")], [B("pf")],
                   start=(g == 0 and j == 0), stop=(g == ng_g - 1 and j == 3),
                   tag=f"fold_{e}_{g}_{j}")

    def dt_stage(e, tc_, pm, bias_col, clip, up):
        DT = T("dtmp")
        act(DT[:, 0:tc_], T(pm)[:, :tc_], AF.Abs, [B(pm)], [B("dtmp")], bias=bias_col)
        act(DT[:, tc_:2 * tc_], DT[:, 0:tc_], AF.Exp, [B("dtmp")], [B("dtmp")], scale=-1.0)
        act(DT[:, tc_:2 * tc_], DT[:, tc_:2 * tc_], AF.Ln, [B("dtmp")], [B("dtmp")], bias=1.0)
        act(DT[:, 2 * tc_:3 * tc_], T(pm)[:, :tc_], AF.Relu, [B(pm)], [B("dtmp")],
            bias=bias_col)
        vtt(DT[:, 0:tc_], DT[:, tc_:2 * tc_], DT[:, 2 * tc_:3 * tc_], ALU.add,
            [B("dtmp")], [B("dtmp")])
        if clip:
            vts(T("dte")[:, :tc_], DT[:, 0:tc_], DT_MAX, DT_MIN, ALU.min, ALU.max,
                [B("dtmp")], [B("dte")])
        else:
            vcopy(T("dte")[:, :tc_], DT[:, 0:tc_], [B("dtmp")], [B("dte")])
        vtt(T("ue")[:, :tc_], T("dte")[:, :tc_], T("xc")[:, e, :tc_], ALU.mult,
            [B("dte"), B("xc")], [B("ue")])
        for j in range(4):
            mm(T("prA")[:, 256 * j:256 * j + tc_], T("rep4")[:, j * 128:(j + 1) * 128],
               T("dte")[:, :tc_], [B("rep4"), B("dte")], [B("prA")], start=True, stop=True)
        pmu = ("pm0", "pm1")[(int(pm[-1]) + 1) % 2]
        for h2 in range(2):
            for j in range(2):
                mm(T(pmu)[:, 256 * j:256 * j + tc_],
                   T("rep4")[:, (2 * h2 + j) * 128:(2 * h2 + j + 1) * 128],
                   T("ue")[:, :tc_], [B("rep4"), B("ue")], [B(pmu)], start=True, stop=True)
            act(T(up)[:, 2 * h2:2 * h2 + 2, :tc_],
                bass.AP(T(pmu), 0, [[512, 128], [256, 2], [1, tc_]]),
                AF.Copy, [B(pmu)], [B(up)])

    def l0_iteration(k, s):
        tp = TP if k < NCH - 1 else tc
        last = (k == NCH - 1)
        xsrc = bass.AP(dram["x"], s * tseq * D + k * tc,
                       [[tseq, 128], [128 * tseq, DBLK], [1, tp]])
        dma("sp", T("x_in")[:, :, :tp], xsrc, [], [B("x_in")], tag=f"ldx_{k}_{s}")
        acopy(T("xn")[:, :, 0:2],
              T("t_xn")[:, s * DBLK * 2:(s + 1) * DBLK * 2].rearrange(
                  "p (e b) -> p e b", e=DBLK),
              [B("t_xn")], [B("xn")])
        acopy(T("xz")[:, :, 0:4].bitcast(F32),
              T("t_xz")[:, s * EBLK * 2:(s + 1) * EBLK * 2].rearrange(
                  "p (e b) -> p e b", e=EBLK),
              [B("t_xz")], [B("xz")])
        ln_stage("x_in", tp, "cinw")

        def rhs_in(kk):
            return T("anb")[:, kk, :tp], [B("anb")]

        def evac_in(o, pm):
            if o < EBLK:
                act(T("xz")[:, o, HB:HB + tp], T(pm)[:, :tp], AF.Copy,
                    [B(pm)], [B("xz")], tag=f"evz_{o}")
            else:
                act(T("sz")[:, o - EBLK, :tp], T(pm)[:, :tp], AF.Copy,
                    [B(pm)], [B("sz")], tag=f"evs_{o}")
        matmul_op("w_inproj", DBLK, 2 * EBLK, rhs_in, evac_in,
                  ("slab8_0", "slab8_1"), "inp")
        if last:
            vmemset(T("xz")[:, :, HB + tc:HB + tc + 2], 0.0, [B("xz")])
        acopy(T("t_xz")[:, s * EBLK * 2:(s + 1) * EBLK * 2].rearrange(
            "p (e b) -> p e b", e=EBLK),
            T("xz")[:, :, tc:tc + 4].bitcast(F32), [B("xz")], [B("t_xz")])
        act(T("sz")[:, :, :tp], T("sz")[:, :, :tp], AF.Silu, [B("sz")], [B("sz")],
            tag="siluz")
        for e in range(EBLK):
            vts(T("xc")[:, e, :], T("xz")[:, e, HB - 1:HB - 1 + tc],
                T("ccw")[:, 4 * e:4 * e + 1], T("ccb")[:, e:e + 1], ALU.mult, ALU.add,
                [B("xz"), B("ccw"), B("ccb")], [B("xc")], tag=f"cv0_{e}")
            for kk in range(1, 4):
                vstt(T("xc")[:, e, :], T("xz")[:, e, HB - 1 + kk:HB - 1 + kk + tc],
                     T("ccw")[:, 4 * e + kk:4 * e + kk + 1], T("xc")[:, e, :],
                     ALU.mult, ALU.add, [B("xz"), B("ccw"), B("xc")], [B("xc")],
                     tag=f"cv{kk}_{e}")
        act(T("xc")[:, :, :], T("xc")[:, :, :], AF.Silu, [B("xc")], [B("xc")],
            tag="silu_xc")
        if cut <= 1:
            return
        dma("pool", T("slabB")[:, :], dram["w_B"][:, :], [], [B("slabB")])
        for kk in range(EBLK):
            mm(T("pb1")[:N0, :tc], T("slabB")[:, kk * N0:(kk + 1) * N0], T("xc")[:, kk, :],
               [B("slabB"), B("xc")], [B("pb1")], start=(kk == 0), stop=(kk == EBLK - 1))
        act(T("bt")[:, :], T("pb1")[:N0, :tc], AF.Copy, [B("pb1")], [B("bt")])
        dma("pool", T("slabB")[:, :], dram["w_C"][:, :], [], [B("slabB")])
        for kk in range(EBLK):
            mm(T("px")[:N0, :tc], T("slabB")[:, kk * N0:(kk + 1) * N0], T("xc")[:, kk, :],
               [B("slabB"), B("xc")], [B("px")], start=(kk == 0), stop=(kk == EBLK - 1))
        act(T("ct")[:, :], T("px")[:N0, :tc], AF.Copy, [B("px")], [B("ct")])
        for g in range(16):
            pb = ("pb1", "px")[g % 2]
            mm(T(pb)[:, :tc], T("sel0")[:, g * 128:(g + 1) * 128], T("bt")[:, :],
               [B("sel0"), B("bt")], [B(pb)], start=True, stop=True)
            act(T("brep")[:, g, :], T(pb)[:, :tc], AF.Copy, [B(pb)], [B("brep")])
            mm(T(pb)[:, :tc], T("sel0")[:, g * 128:(g + 1) * 128], T("ct")[:, :],
               [B("sel0"), B("ct")], [B(pb)], start=True, stop=True)
            act(T("crep")[:, g, :], T(pb)[:, :tc], AF.Copy, [B(pb)], [B("crep")])
        if cut <= 2:
            return
        for e in range(EBLK):
            sl = ("slab16_0", "slab16_1")[e % 2]
            dma("pool", T(sl)[:, :], wslab_ap("w_dt", e, EBLK), [], [B(sl)], tag=f"wdt_{e}")
            pm = ("pm0", "pm1")[e % 2]
            for kk in range(EBLK):
                mm(T(pm)[:, :tc], T(sl)[:, kk * 128:(kk + 1) * 128], T("xc")[:, kk, :],
                   [B(sl), B("xc")], [B(pm)], start=(kk == 0), stop=(kk == EBLK - 1))
            up = f"ur4_{e % 2}"
            dt_stage(e, tc, pm, T("cdtb")[:, e:e + 1], True, up)
            scan_block(e, s, tc, 0, up)
            vtt(T("yz")[:, e, :], T("pf")[:, :tc], T("sz")[:, e, :tc], ALU.mult,
                [B("pf"), B("sz")], [B("yz")], tag=f"yz_{e}")
        if cut <= 3:
            return

        def rhs_out(kk):
            return T("yz")[:, kk, :], [B("yz")]

        def evac_out(o, pm):
            vtt(T("xatt")[:, o, :], T(pm)[:, :tc], T("xn")[:, o, 2:2 + tc], ALU.add,
                [B(pm), B("xn")], [B("xatt")], tag=f"evo_{o}")
        matmul_op("w_outp", EBLK, DBLK, rhs_out, evac_out,
                  ("slab16_0", "slab16_1"), "outp")
        if cut <= 4:
            return
        act(T("scr")[:, :, :tc], T("xatt")[:, :, :tc], AF.Square, [B("xatt")], [B("scr")])
        for d in range(DBLK):
            mm(T("pb1")[:1, :tc], T("ones_c")[:, :], T("scr")[:, d, :tc],
               [B("scr"), B("ones_c")], [B("pb1")], start=(d == 0), stop=(d == DBLK - 1))
        R = T("rows")
        act(R[:1, 0:tc], T("pb1")[:1, :tc], AF.Ln, [B("pb1")], [B("rows")],
            bias=EPS, scale=1.0 / D)
        act(R[:1, 0:tc], R[:1, 0:tc], AF.Exp, [B("rows")], [B("rows")], scale=-0.5)
        mm(T("pb0")[:, :tc], T("ones_r")[:1, :], R[:1, 0:tc], [B("rows"), B("ones_r")],
           [B("pb0")], start=True, stop=True)
        for d in range(DBLK):
            vstt(T("n2b")[:, d, :], T("xatt")[:, d, :], T("cffw")[:, d:d + 1],
                 T("pb0")[:, :tc], ALU.mult, ALU.mult,
                 [B("xatt"), B("cffw"), B("pb0")], [B("n2b")], tag=f"n2_{d}")

        def rhs_n2(kk):
            return T("n2b")[:, kk, :], [B("n2b")]

        def evac_gate(o, pm):
            act(T("gu")[:, o, :], T(pm)[:, :tc], AF.Silu, [B(pm)], [B("gu")],
                tag=f"evg_{o}")
        matmul_op("w_gate", DBLK, EBLK, rhs_n2, evac_gate, ("slab8_0", "slab8_1"), "gate")

        def evac_up(o, pm):
            vtt(T("gu")[:, o, :], T(pm)[:, :tc], T("gu")[:, o, :], ALU.mult,
                [B(pm), B("gu")], [B("gu")], tag=f"evu_{o}")
        matmul_op("w_up", DBLK, EBLK, rhs_n2, evac_up, ("slab8_0", "slab8_1"), "up")

        def rhs_gu(kk):
            return T("gu")[:, kk, :], [B("gu")]

        def evac_down(o, pm):
            vtt(T("xatt")[:, o, :], T(pm)[:, :tc], T("xatt")[:, o, :], ALU.add,
                [B(pm), B("xatt")], [B("xatt")], tag=f"evd_{o}")
        matmul_op("w_down", EBLK, DBLK, rhs_gu, evac_down,
                  ("slab16_0", "slab16_1"), "down")
        vtt(T("xatt")[:, :, :], T("xatt")[:, :, :], T("x_in")[:, :, :tc], ALU.add,
            [B("xatt"), B("x_in")], [B("xatt")])
        if cut <= 5:
            return
        acopy(T("t_xn")[:, s * DBLK * 2:(s + 1) * DBLK * 2].rearrange(
            "p (e b) -> p e b", e=DBLK),
            T("xn")[:, :, tc:tc + 2], [B("xn")], [B("t_xn")])
        for (mu, om, wname, tag2) in [("cmur", "comr", "w_r", "r"),
                                      ("cmuk", "comk", "w_k", "kk")]:
            for d in range(DBLK):
                vts(T("gtmp")[:, :], T("xn")[:, d, 1:1 + tc], T(om)[:, d:d + 1], None,
                    ALU.mult, None, [B("xn"), B(om)], [B("gtmp")])
                vstt(T("mixb")[:, d, :], T("xn")[:, d, 2:2 + tc], T(mu)[:, d:d + 1],
                     T("gtmp")[:, :], ALU.mult, ALU.add,
                     [B("xn"), B(mu), B("gtmp")], [B("mixb")], tag=f"mx_{tag2}_{d}")

            def rhs_mix(kk_):
                return T("mixb")[:, kk_, :], [B("mixb")]

            if tag2 == "r":
                def evac_r(o, pm):
                    act(T("gtmp")[:, :], T(pm)[:, :tc], AF.Tanh, [B(pm)], [B("gtmp")],
                        scale=0.5)
                    vts(T("rb")[:, o, :], T("gtmp")[:, :], 0.5, 0.5, ALU.mult, ALU.add,
                        [B("gtmp")], [B("rb")], tag=f"evr_{o}")
                matmul_op("w_r", DBLK, DBLK, rhs_mix, evac_r, ("slab8_0", "slab8_1"), "wr")
            else:
                def evac_k(o, pm):
                    act(T("kact")[:, o, :], T(pm)[:, :tc], AF.Relu, [B(pm)], [B("kact")],
                        tag=f"evk_{o}")
                matmul_op("w_k", DBLK, DBLK, rhs_mix, evac_k, ("slab8_0", "slab8_1"), "wk")
        act(T("kact")[:, :, :], T("kact")[:, :, :], AF.Square, [B("kact")], [B("kact")])

        def rhs_ka(kk):
            return T("kact")[:, kk, :], [B("kact")]

        def evac_v(o, pm):
            vtt(T("gtf")[:, :], T(pm)[:, :tc], T("rb")[:, o, :], ALU.mult,
                [B(pm), B("rb")], [B("gtf")])
            vtt(T("xatt")[:, o, :], T("xatt")[:, o, :], T("gtf")[:, :], ALU.add,
                [B("gtf"), B("xatt")], [B("xatt")], tag=f"evv_{o}")
        matmul_op("w_v", DBLK, DBLK, rhs_ka, evac_v, ("slab8_0", "slab8_1"), "wv")

    def l1_iteration(k, s):
        ln_stage("xatt", tc, None)
        acopy(T("xz")[:, :, 0:4].bitcast(F32),
              T("t_xz1")[:, s * EBLK * 2:(s + 1) * EBLK * 2].rearrange(
                  "p (e b) -> p e b", e=EBLK),
              [B("t_xz1")], [B("xz")])

        def rhs_in(kk):
            return T("anb")[:, kk, :tc], [B("anb")]

        def evac_in(o, pm):
            if o < EBLK:
                act(T("xz")[:, o, HB:HB + tc], T(pm)[:, :tc], AF.Copy, [B(pm)], [B("xz")])
            else:
                act(T("sz")[:, o - EBLK, :tc], T(pm)[:, :tc], AF.Copy, [B(pm)], [B("sz")])
        matmul_op("w1_in", DBLK, 2 * EBLK, rhs_in, evac_in, ("slab8_0", "slab8_1"), "l1in")
        acopy(T("t_xz1")[:, s * EBLK * 2:(s + 1) * EBLK * 2].rearrange(
            "p (e b) -> p e b", e=EBLK),
            T("xz")[:, :, tc:tc + 4].bitcast(F32), [B("xz")], [B("t_xz1")])
        act(T("sz")[:, :, :tc], T("sz")[:, :, :tc], AF.Silu, [B("sz")], [B("sz")])
        for e in range(EBLK):
            vts(T("xc")[:, e, :], T("xz")[:, e, 1:1 + tc],
                T("c1cw")[:, 4 * e:4 * e + 1], T("c1cb")[:, e:e + 1], ALU.mult, ALU.add,
                [B("xz"), B("c1cw"), B("c1cb")], [B("xc")])
            for kk in range(1, 4):
                vstt(T("xc")[:, e, :], T("xz")[:, e, 1 + kk:1 + kk + tc],
                     T("c1cw")[:, 4 * e + kk:4 * e + kk + 1], T("xc")[:, e, :],
                     ALU.mult, ALU.add, [B("xz"), B("c1cw"), B("xc")], [B("xc")])
        act(T("xc")[:, :, :], T("xc")[:, :, :], AF.Silu, [B("xc")], [B("xc")])
        dma("pool", T("slabX1")[:, :], dram["w1_x"][:, :], [], [B("slabX1")])
        for kk in range(EBLK):
            mm(T("pb1")[:N1, :tc], T("slabX1")[:, kk * N1:(kk + 1) * N1], T("xc")[:, kk, :],
               [B("slabX1"), B("xc")], [B("pb1")], start=(kk == 0), stop=(kk == EBLK - 1))
        act(T("bq1")[:, :], T("pb1")[:N1, :tc], AF.Copy, [B("pb1")], [B("bq1")])
        for g in range(4):
            pb = ("pb1", "px")[g % 2]
            mm(T(pb)[:, :tc], T("sel1")[:, g * 128:(g + 1) * 128], T("bq1")[:, :],
               [B("sel1"), B("bq1")], [B(pb)], start=True, stop=True)
            act(T("brep1")[:, g, :], T(pb)[:, :tc], AF.Copy, [B(pb)], [B("brep1")])
        for e in range(EBLK):
            sl = ("slab16_0", "slab16_1")[e % 2]
            dma("pool", T(sl)[:, :], wslab_ap("w1_dt", e, EBLK), [], [B(sl)],
                tag=f"wdt1_{e}")
            pm = ("pm0", "pm1")[e % 2]
            for kk in range(EBLK):
                mm(T(pm)[:, :tc], T(sl)[:, kk * 128:(kk + 1) * 128], T("xc")[:, kk, :],
                   [B(sl), B("xc")], [B(pm)], start=(kk == 0), stop=(kk == EBLK - 1))
            up = f"ur4_{e % 2}"
            dt_stage(e, tc, pm, T("c1db")[:, e:e + 1], False, up)
            scan_block(e, s, tc, 1, up)
            vstt(T("gtf")[:, :], T("xc")[:, e, :], T("c1D")[:, e:e + 1], T("pf")[:, :tc],
                 ALU.mult, ALU.add, [B("xc"), B("c1D"), B("pf")], [B("gtf")], tag=f"y1_{e}")
            vtt(T("yz")[:, e, :], T("gtf")[:, :], T("sz")[:, e, :tc], ALU.mult,
                [B("gtf"), B("sz")], [B("yz")])

        def rhs_out(kk):
            return T("yz")[:, kk, :], [B("yz")]

        def evac_out(o, pm):
            vtt(T("x2")[:, o, :], T(pm)[:, :tc], T("xatt")[:, o, :], ALU.add,
                [B(pm), B("xatt")], [B("x2")], tag=f"evo1_{o}")
        matmul_op("w1_out", EBLK, DBLK, rhs_out, evac_out,
                  ("slab16_0", "slab16_1"), "l1out")
        ntt = tc // 128
        R = T("rows")
        for tt in range(ntt):
            for d in range(DBLK):
                transp(T("px")[:, :].bitcast(BF16)[:, :128],
                       T("anb")[:, d, tt * 128:(tt + 1) * 128],
                       [B("anb")], [B("px")])
                vreduce(T("amax")[:, 8 + 8 * tt + d:9 + 8 * tt + d],
                        T("px")[:, :].bitcast(BF16)[:, :128],
                        ALU.max, [B("px")], [B("amax")], absval=True)
        for tt in range(ntt):
            vreduce(T("amax")[:, tt:tt + 1], T("amax")[:, 8 + 8 * tt:16 + 8 * tt],
                    ALU.max, [B("amax")], [B("amax")])
        for tt in range(ntt):
            vts(T("amax")[:, tt:tt + 1], T("amax")[:, tt:tt + 1], 1e-5, None, ALU.max,
                None, [B("amax")], [B("amax")])
        for tt in range(ntt):
            transp(T("px")[:1, :128], T("amax")[:, tt:tt + 1], [B("amax")], [B("px")],
                   f32=True)
            act(R[:1, 1024 + tt * 128:1024 + (tt + 1) * 128], T("px")[:1, :128],
                AF.Copy, [B("px")], [B("rows")])
        vrecip(R[:1, 1280:1280 + tc], R[:1, 1024:1024 + tc], [B("rows")], [B("rows")])
        vts(R[:1, 1280:1280 + tc], R[:1, 1280:1280 + tc], 127.0, None, ALU.mult, None,
            [B("rows")], [B("rows")])
        vts(R[:1, 1536:1536 + tc], R[:1, 1024:1024 + tc], 1.0 / 127.0, None, ALU.mult,
            None, [B("rows")], [B("rows")])
        mm(T("pb0")[:, :tc], T("ones_r")[:1, :], R[:1, 1280:1280 + tc],
           [B("rows"), B("ones_r")], [B("pb0")], start=True, stop=True)
        vtt(T("scr")[:, :, :tc], T("xn")[:, :, 2:2 + tc],
            T("pb0")[:, :tc].unsqueeze(1).broadcast_to([128, DBLK, tc]), ALU.mult,
            [B("xn"), B("pb0")], [B("scr")])
        vts(T("scr")[:, :, :tc], T("scr")[:, :, :tc], RND_C, -RND_C, ALU.add, ALU.add,
            [B("scr")], [B("scr")])
        vts(T("xq")[:, :, :], T("scr")[:, :, :tc], 127.0, -128.0, ALU.min, ALU.max,
            [B("scr")], [B("xq")])
        for wi, dst in [(0, 0), (1, 1)]:
            vts(R[:1, 1024:1024 + tc], R[:1, 1536:1536 + tc],
                T("wsc")[:1, 8 + wi:9 + wi], None, ALU.mult, None,
                [B("rows"), B("wsc")], [B("rows")])
            mm(T("pb0")[:, :tc], T("ones_r")[:1, :], R[:1, 1024:1024 + tc],
               [B("rows"), B("ones_r")], [B("pb0")], start=True, stop=True)
            act(T("sinv")[:, dst * tc:(dst + 1) * tc], T("pb0")[:, :tc], AF.Copy,
                [B("pb0")], [B("sinv")])

        def rhs_xq(kk):
            return T("xq")[:, kk, :], [B("xq")]

        def evac_g1(o, pm):
            vtt(T("gtf")[:, :], T(pm)[:, :tc], T("sinv")[:, 0:tc], ALU.mult,
                [B(pm), B("sinv")], [B("gtf")])
            act(T("gu")[:, o, :], T("gtf")[:, :], AF.Silu, [B("gtf")], [B("gu")],
                tag=f"evg1_{o}")
        matmul_op("s_wq_gate", DBLK, 2 * EBLK, rhs_xq, evac_g1,
                  ("slab8_0", "slab8_1"), "l1g", wq=True)

        def evac_u1(o, pm):
            vtt(T("gtf")[:, :], T(pm)[:, :tc], T("sinv")[:, tc:2 * tc], ALU.mult,
                [B(pm), B("sinv")], [B("gtf")])
            vtt(T("gu")[:, o, :], T("gu")[:, o, :], T("gtf")[:, :], ALU.mult,
                [B("gtf"), B("gu")], [B("gu")], tag=f"evu1_{o}")
        matmul_op("s_wq_up", DBLK, 2 * EBLK, rhs_xq, evac_u1,
                  ("slab8_0", "slab8_1"), "l1u", wq=True)
        for grp in range(4):
            for tt in range(ntt):
                for d8 in range(8):
                    d = grp * 8 + d8
                    transp(T("px")[:, :].bitcast(BF16)[:, :128],
                           T("gu")[:, d, tt * 128:(tt + 1) * 128],
                           [B("gu")], [B("px")])
                    vreduce(T("amax")[:, 8 + 8 * tt + d8:9 + 8 * tt + d8],
                            T("px")[:, :].bitcast(BF16)[:, :128],
                            ALU.max, [B("px")], [B("amax")], absval=True)
            vreduce(T("amax")[:, 24 + grp:25 + grp], T("amax")[:, 8:16],
                    ALU.max, [B("amax")], [B("amax")])
            vreduce(T("amax")[:, 28 + grp:29 + grp], T("amax")[:, 16:24],
                    ALU.max, [B("amax")], [B("amax")])
        vreduce(T("amax")[:, 0:1], T("amax")[:, 24:28], ALU.max, [B("amax")],
                [B("amax")])
        vreduce(T("amax")[:, 1:2], T("amax")[:, 28:32], ALU.max, [B("amax")],
                [B("amax")])
        for tt in range(ntt):
            vts(T("amax")[:, tt:tt + 1], T("amax")[:, tt:tt + 1], 1e-5, None, ALU.max,
                None, [B("amax")], [B("amax")])
        for tt in range(ntt):
            transp(T("px")[:1, :128], T("amax")[:, tt:tt + 1], [B("amax")], [B("px")],
                   f32=True)
            act(R[:1, 1024 + tt * 128:1024 + (tt + 1) * 128], T("px")[:1, :128],
                AF.Copy, [B("px")], [B("rows")])
        vrecip(R[:1, 1280:1280 + tc], R[:1, 1024:1024 + tc], [B("rows")], [B("rows")])
        vts(R[:1, 1280:1280 + tc], R[:1, 1280:1280 + tc], 127.0, None, ALU.mult, None,
            [B("rows")], [B("rows")])
        vts(R[:1, 1536:1536 + tc], R[:1, 1024:1024 + tc], 1.0 / 127.0, None, ALU.mult,
            None, [B("rows")], [B("rows")])
        mm(T("pb0")[:, :tc], T("ones_r")[:1, :], R[:1, 1280:1280 + tc],
           [B("rows"), B("ones_r")], [B("pb0")], start=True, stop=True)
        for grp in range(4):
            sl8 = slice(grp * 8, (grp + 1) * 8)
            vtt(T("scr")[:, :, :tc], T("gu")[:, sl8, :],
                T("pb0")[:, :tc].unsqueeze(1).broadcast_to([128, 8, tc]), ALU.mult,
                [B("gu"), B("pb0")], [B("scr")])
            vts(T("scr")[:, :, :tc], T("scr")[:, :, :tc], RND_C, -RND_C, ALU.add, ALU.add,
                [B("scr")], [B("scr")])
            vts(T("gu")[:, sl8, :], T("scr")[:, :, :tc], 127.0, -128.0, ALU.min, ALU.max,
                [B("scr")], [B("gu")])
        vts(R[:1, 1024:1024 + tc], R[:1, 1536:1536 + tc], T("wsc")[:1, 10:11], None,
            ALU.mult, None, [B("rows"), B("wsc")], [B("rows")])
        mm(T("pb0")[:, :tc], T("ones_r")[:1, :], R[:1, 1024:1024 + tc],
           [B("rows"), B("ones_r")], [B("pb0")], start=True, stop=True)
        act(T("sinv")[:, 2 * tc:3 * tc], T("pb0")[:, :tc], AF.Copy, [B("pb0")],
            [B("sinv")])

        def rhs_gu1(kk):
            return T("gu")[:, kk, :], [B("gu")]

        def evac_d1(o, pm):
            vtt(T("gtf")[:, :], T(pm)[:, :tc], T("sinv")[:, 2 * tc:3 * tc], ALU.mult,
                [B(pm), B("sinv")], [B("gtf")])
            vtt(T("x2")[:, o, :], T("x2")[:, o, :], T("gtf")[:, :], ALU.add,
                [B("gtf"), B("x2")], [B("x2")], tag=f"evd1_{o}")
        for o in range(DBLK):
            pm = ("pm0", "pm1")[o % 2]
            for h in range(2):
                sl = ("slab16_0", "slab16_1")[h]
                dma("sp", T(sl)[:, :], wslab_ap("s_wq_down", o, EBLK, ktot=2 * EBLK,
                                                koff=h * EBLK),
                    [B("d_wq2")], [B(sl)], tag=f"w_l1d_{o}_{h}")
                for kk in range(EBLK):
                    mm(T(pm)[:, :tc], T(sl)[:, kk * 128:(kk + 1) * 128],
                       T("gu")[:, h * EBLK + kk, :], [B(sl), B("gu")], [B(pm)],
                       start=(h == 0 and kk == 0), stop=(h == 1 and kk == EBLK - 1),
                       tag=f"mm_l1d_{o}_{h}_{kk}")
            evac_d1(o, pm)
        dst = bass.AP(dram["out"], s * tseq * D + k * tc,
                      [[tseq, 128], [128 * tseq, DBLK], [1, tc]])
        dma("sp", dst, T("x2")[:, :, :], [B("x2")], [], tag=f"st_{k}_{s}", skey="out")
        n_store[0] += 1

    def dbg_dump():
        for nm in debug:
            t = sb_t[nm]
            shape = list(t.shape) if hasattr(t, "shape") else None
            dd = nc.dram_tensor("dbg_" + nm, shape, t.dtype, kind="ExternalOutput")
            ap = t[:, :] if len(shape) == 2 else t[:, :, :] if len(shape) == 3 else t[:, :, :, :]
            dma("pool", dd[:, :] if len(shape) == 2 else dd[:, :, :] if len(shape) == 3
                else dd[:, :, :, :], ap, [bufs[nm]], [], tag=f"dbg_{nm}", skey="out")

    for k in range(NCH):
        for s in range(seqs):
            if 0 in layers:
                k_dbg[0] = k
                if k == dbg_k:
                    dbg_scan[0] = 0 if s == 0 else 99
                l0_iteration(k, s)
                if debug and k == dbg_k and s == 0:
                    dbg_dump()
            if 1 in layers:
                if 0 not in layers:
                    xsrc = bass.AP(dram["x"], s * tseq * D + k * tc,
                                   [[tseq, 128], [128 * tseq, DBLK], [1, tc]])
                    dma("sp", T("xatt")[:, :, :], xsrc, [], [B("xatt")],
                        tag=f"ldx1_{k}_{s}")
                l1_iteration(k, s)
                if debug and 0 not in layers and k == dbg_k and s == 0:
                    dbg_dump()
            elif 0 in layers:
                dst = bass.AP(dram["out"], s * tseq * D + k * tc,
                              [[tseq, 128], [128 * tseq, DBLK], [1, tc]])
                dma("sp", dst, T("xatt")[:, :, :], [B("xatt")], [], tag=f"st_{k}_{s}",
                    skey="out")
                n_store[0] += 1
    for op in em.ops["sp"]:
        if op.tag.startswith("st_"):
            op.must_inc = True

    with es:
        inc_count = em.assign()
        sems = {}
        for e in ("pe", "act", "dve"):
            sems[e] = es.enter_context(nc.semaphore(f"sem_{e}"))
        for k in em.skeys:
            sems[k] = es.enter_context(nc.semaphore(f"sem_{k}"[:31]))
        fin_by_eng = {"sp": [], "pool": []}
        for e in ("sp", "pool"):
            keys = sorted({op.skey for op in em.ops[e]})
            for k in keys:
                fin_by_eng[e].append((k, inc_count.get(k, 0)))
        with nc.Block() as block:
            @block.tensor
            def _(t):
                em.lower_engine("pe", t, sems)

            @block.scalar
            def _(a):
                em.lower_engine("act", a, sems)

            @block.vector
            def _(v):
                em.lower_engine("dve", v, sems)

            @block.sync
            def _(sp):
                em.lower_engine("sp", sp, sems, final_waits=fin_by_eng["sp"])

            @block.gpsimd
            def _(g):
                em.lower_engine("pool", g, sems, final_waits=fin_by_eng["pool"])
    return nc, {e: len(v) for e, v in em.ops.items()}


def _pack_weights(inputs):
    f = lambda a: np.ascontiguousarray(np.asarray(a, dtype=np.float32))
    ins = {k: f(v) for k, v in inputs.items()}

    def col128(v):
        v = f(v)
        return np.ascontiguousarray(v.reshape(-1, 128).T)

    d = {}
    d["w_inproj"] = pack_lhsT(ins["L0_attn_in_proj_W"])
    d["w_dt"] = pack_lhsT(ins["L0_attn_dt_W"])
    d["w_B"] = pack_lhsT_narrow(ins["L0_attn_B_W"])
    d["w_C"] = pack_lhsT_narrow(ins["L0_attn_C_W"])
    d["w_outp"] = pack_lhsT(ins["L0_attn_out_proj_W"])
    d["w_gate"] = pack_lhsT(ins["L0_attn_gate_W"])
    d["w_up"] = pack_lhsT(ins["L0_attn_up_W"])
    d["w_down"] = pack_lhsT(ins["L0_attn_down_W"])
    d["w_r"] = pack_lhsT(ins["L0_ffn_W_r"].T)
    d["w_k"] = pack_lhsT(ins["L0_ffn_W_k"].T)
    d["w_v"] = pack_lhsT(ins["L0_ffn_W_v"].T)
    cw = ins["L0_attn_conv_W"][:, 0, :]
    d["c_conv_w"] = np.ascontiguousarray(
        cw.T.reshape(EBLK, 128, 4).transpose(1, 0, 2).reshape(128, EBLK * 4))
    d["c_conv_b"] = col128(ins["L0_attn_conv_b"])
    d["c_dt_b"] = col128(ins["L0_attn_dt_b"])
    d["c_in_norm_w"] = col128(ins["L0_attn_input_norm_w"])
    d["c_ffn_norm_w"] = col128(ins["L0_attn_ffn_norm_w"])
    d["c_mu_r"] = col128(ins["L0_ffn_mu_r"])
    d["c_mu_k"] = col128(ins["L0_ffn_mu_k"])
    a0 = ins["L0_attn_A_log"].reshape(N0)
    p_ = np.arange(128)
    d["a0_logc"] = np.ascontiguousarray(
        a0[4 * np.arange(16)[None, :] + (p_ // 32)[:, None]])
    d["w1_in"] = pack_lhsT(ins["L1_attn_W_in"].T)
    d["w1_x"] = pack_lhsT_narrow(ins["L1_attn_W_x"][N1:2 * N1, :].T)
    d["w1_dt"] = pack_lhsT(ins["L1_attn_W_dt"].T)
    d["w1_out"] = pack_lhsT(ins["L1_attn_W_out"].T)
    d["w1_gatef"] = pack_lhsT(ins["L1_ffn_w_gate"].T)
    d["w1_upf"] = pack_lhsT(ins["L1_ffn_w_up"].T)
    d["w1_downf"] = pack_lhsT(ins["L1_ffn_w_down"].T)
    cw1 = ins["L1_attn_W_conv"][:, 0, :]
    d["c1_conv_w"] = np.ascontiguousarray(
        cw1.reshape(EBLK, 128, 4).transpose(1, 0, 2).reshape(128, EBLK * 4))
    d["c1_conv_b"] = col128(ins["L1_attn_b_conv"])
    d["c1_dt_b"] = col128(ins["L1_attn_b_dt"])
    d["c1_D"] = col128(ins["L1_attn_D"])
    a1 = ins["L1_attn_A_log"]
    cidx = np.arange(EBLK * 16)
    e_, j_, g_ = cidx // 16, (cidx // 4) % 4, cidx % 4
    ch = e_[None, :] * 128 + j_[None, :] * 32 + (p_ % 32)[:, None]
    nn = 4 * g_[None, :] + (p_ // 32)[:, None]
    d["a1_logc"] = np.ascontiguousarray(a1[ch, nn])
    rep4 = np.zeros((128, 4, 128), np.float32)
    for j in range(4):
        for p_ in range(128):
            rep4[32 * j + (p_ % 32), j, p_] = 1.0
    d["m_rep4"] = rep4.reshape(128, 512)
    sel0 = np.zeros((N0, 16, 128), np.float32)
    for g in range(16):
        for p_ in range(128):
            sel0[4 * g + p_ // 32, g, p_] = 1.0
    d["m_sel0"] = sel0.reshape(N0, 16 * 128)
    sel1 = np.zeros((N1, 4, 128), np.float32)
    for g in range(4):
        for p_ in range(128):
            sel1[4 * g + p_ // 32, g, p_] = 1.0
    d["m_sel1"] = sel1.reshape(N1, 4 * 128)
    p4 = np.zeros((128, 4, 128), np.float32)
    for j in range(4):
        for p_ in range(128):
            p4[p_, j, 32 * j + (p_ % 32)] = 1.0
    d["m_p4"] = p4.reshape(128, 512)
    d["m_ones_r"] = np.ones((1, 128), np.float32)
    d["m_ones_c"] = np.ones((128, 1), np.float32)
    d["m_ident"] = np.eye(128, dtype=np.float32)
    return d


def _prep_inputs(inputs, shard):
    d = _pack_weights(inputs)
    x = np.asarray(inputs["L0_attn_x"], dtype=np.float32)
    in_maps = []
    for c in range(shard):
        m = dict(d)
        m["x"] = np.ascontiguousarray(x[c * SEQS:(c + 1) * SEQS].transpose(0, 2, 1))
        in_maps.append(m)
    return in_maps


_NC_CACHE = {}


def kernel(**inputs):
    key = "full"
    if key not in _NC_CACHE:
        _NC_CACHE[key] = build_program()
    nc, counts = _NC_CACHE[key]
    in_maps = _prep_inputs(inputs, NCORES)
    res = run_bass_kernel_spmd(nc, in_maps, core_ids=list(range(NCORES)))
    out = np.concatenate([np.asarray(r["out"]).transpose(0, 2, 1)
                          for r in res.results], axis=0)
    return np.ascontiguousarray(out).astype(np.float32)
